# version 03
# baseline (speedup 1.0000x reference)
import numpy as np
import ml_dtypes
import concourse.bass as bass
import concourse.mybir as mybir
from concourse.bass_utils import run_bass_kernel_spmd
from contextlib import ExitStack

F32 = mybir.dt.float32
BF16 = mybir.dt.bfloat16
AF = mybir.ActivationFunctionType
AX = mybir.AxisListType
OP = mybir.AluOpType

D = 1024
H = 2048
NEXP = 8
NCORE = 8
NTOK = 2 * 16384
NC = NTOK // NCORE
NSUP = 2048
SUPS = NC // NSUP
CHUNKS = NSUP // 128
NR = NSUP // 512
DC = D // 128
HC = H // 128


def build_program(debug=False):
    nc = bass.Bass("TRN2", target_bir_lowering=False)

    x_h = nc.dram_tensor("x", [NC, D], F32, kind="ExternalInput").ap()
    wg_h = nc.dram_tensor("wg", [D, NEXP], F32, kind="ExternalInput").ap()
    bg_h = nc.dram_tensor("bg", [NEXP], F32, kind="ExternalInput").ap()
    w1_h = nc.dram_tensor("w1", [NEXP, D, H], F32, kind="ExternalInput").ap()
    b1_h = nc.dram_tensor("b1", [NEXP, H], F32, kind="ExternalInput").ap()
    w2_h = nc.dram_tensor("w2", [NEXP, H, D], F32, kind="ExternalInput").ap()
    b2_h = nc.dram_tensor("b2", [NEXP, D], F32, kind="ExternalInput").ap()
    wet_h = nc.dram_tensor("wet", [NEXP, D, D], F32, kind="ExternalInput").ap()
    be_h = nc.dram_tensor("be", [NEXP, D], F32, kind="ExternalInput").ap()
    wr_h = nc.dram_tensor("wr", [NEXP, D], F32, kind="ExternalInput").ap()
    wrt_h = nc.dram_tensor("wrt", [D, NEXP], F32, kind="ExternalInput").ap()
    br_h = nc.dram_tensor("br", [NEXP], F32, kind="ExternalInput").ap()
    idb_h = nc.dram_tensor("idb", [128, 128], BF16, kind="ExternalInput").ap()
    out_h = nc.dram_tensor("out", [NC, D], F32, kind="ExternalOutput").ap()
    if debug:
        dbg_xn_h = nc.dram_tensor("dbg_xn", [128, DC * NSUP], BF16, kind="ExternalOutput").ap()
        dbg_s_h = nc.dram_tensor("dbg_s", [128, CHUNKS * 16], BF16, kind="ExternalOutput").ap()
        dbg_acc0_h = nc.dram_tensor("dbg_acc0", [128, CHUNKS * D], F32, kind="ExternalOutput").ap()
        dbg_hid_h = nc.dram_tensor("dbg_hid", [128, HC * 512], BF16, kind="ExternalOutput").ap()
        dbg_accA_h = nc.dram_tensor("dbg_accA", [128, CHUNKS * D], F32, kind="ExternalOutput").ap()

    ctx = ExitStack()
    EN = ctx.enter_context
    xn_fm = EN(nc.sbuf_tensor("xn_fm", [128, DC * NSUP], BF16))
    acc = EN(nc.sbuf_tensor("acc", [128, CHUNKS * D], F32))
    wunion = EN(nc.sbuf_tensor("wunion", [128, 32768], BF16))
    w1sb = wunion[:, 0:16384]
    w2sb = wunion[:, 16384:32768]
    weT = [wunion[:, 0:8192], wunion[:, 8192:16384]]
    hid_sb = EN(nc.sbuf_tensor("hid_sb", [128, HC * 512], BF16))

    s_sb = EN(nc.sbuf_tensor("s_sb", [128, CHUNKS * 16], BF16))
    srT = EN(nc.sbuf_tensor("srT", [16, NSUP], BF16))
    b1sb = EN(nc.sbuf_tensor("b1sb", [128, NEXP * HC], F32))
    bias_stack = EN(nc.sbuf_tensor("bias_stack", [16, D], BF16))
    gate_w = EN(nc.sbuf_tensor("gate_w", [128, DC * 32], BF16))
    ident = EN(nc.sbuf_tensor("ident", [128, 128], BF16))
    ones_row = EN(nc.sbuf_tensor("ones_row", [1, 128], BF16))
    bgbr = EN(nc.sbuf_tensor("bgbr", [1, 16], BF16))
    consts = EN(nc.sbuf_tensor("consts", [128, 8], F32))
    xst = EN(nc.sbuf_tensor("xst", [128, 2 * D], F32))
    outst = EN(nc.sbuf_tensor("outst", [128, 2 * D], F32))
    xnf32 = outst[:, 0:D]
    xhit = EN(nc.sbuf_tensor("xhit", [128, D], BF16))
    xlot = EN(nc.sbuf_tensor("xlot", [128, D], BF16))
    xlofm = EN(nc.sbuf_tensor("xlofm", [128, D], BF16))
    lnw = EN(nc.sbuf_tensor("lnw", [128, 96], F32))
    gw48 = EN(nc.sbuf_tensor("gw48", [128, 48], F32))
    rwbuf = EN(nc.sbuf_tensor("rwbuf", [128, 48], F32))
    gwf = EN(nc.sbuf_tensor("gwf", [128, DC * 8], F32))
    wrtf = EN(nc.sbuf_tensor("wrtf", [128, DC * 8], F32))
    b1st = hid_sb[0:16, 0:NEXP * 128]
    bgbrst = lnw[0:1, 80:96]
    psA = EN(nc.psum_tensor("psA", [128, 1024], F32))
    psB = EN(nc.psum_tensor("psB", [128, 1024], F32))
    psC = EN(nc.psum_tensor("psC", [128, 1024], F32))
    psD = EN(nc.psum_tensor("psD", [128, 1024], F32))
    sem = {k: EN(nc.semaphore("sem_" + k)) for k in ("w", "x", "pe", "act", "dve", "st")}

    ops = {k: [] for k in ("te", "ve", "sc", "gp", "sy")}
    cnt = {k: 0 for k in ("w", "x", "pe", "act", "dve", "st")}

    def I(eng, fn):
        ops[eng].append(["call", fn])

    def W(eng, sname, val):
        if val and val > 0:
            ops[eng].append(["wait", sem[sname], val])

    def INC(eng, sname, amt=1):
        cnt[sname] += amt
        last = ops[eng][-1]
        assert last[0] == "call", f"INC after non-call on {eng}"
        fn = last[1]
        last[0] = "inccall"
        last.append(sem[sname])
        last.append(amt)
        return cnt[sname]

    def DMA(eng, dst, src, sname, **kw):
        cnt[sname] += 16
        ops[eng].append(["dma", dst, src, sem[sname], kw])
        return cnt[sname]

    spacer_i = [0]

    def SP():
        c = 4 + spacer_i[0] % 4
        spacer_i[0] += 1
        I("ve", lambda E, c=c: E.tensor_copy(consts[:, c : c + 1], consts[:, 0:1]))

    w1v = w1sb.rearrange("p (dc h) -> p dc h", h=H)
    w2v = w2sb.rearrange("p (hc d) -> p hc d", d=D)

    def dma_w1(e, half):
        return DMA("gp", w1v[:, :, half * 1024 : (half + 1) * 1024],
                   w1_h[e].rearrange("(dc p) h -> p dc h", p=128)[:, :, half * 1024 : (half + 1) * 1024],
                   "w", max_dma_last_dim=4096)

    def dma_w2(e):
        return DMA("gp", w2v, w2_h[e].rearrange("(hc p) d -> p hc d", p=128),
                   "w", max_dma_last_dim=4096)

    def dma_wet(e):
        return DMA("gp", weT[e % 2].rearrange("p (dc o) -> p dc o", o=D),
                   wet_h[e].rearrange("(dc p) o -> p dc o", p=128), "w", max_dma_last_dim=4096)

    I("ve", lambda E: E.memset(consts[:, 0:1], 1e-5))
    I("ve", lambda E: E.memset(consts[:, 1:2], -0.5))
    I("ve", lambda E: E.memset(consts[:, 2:3], -1.0))
    I("ve", lambda E: E.memset(consts[:, 3:4], 1.0 / 1024.0))
    I("ve", lambda E: E.memset(ones_row[:, :], 1.0))
    c_consts = INC("ve", "dve")

    c_ident = DMA("sy", ident[:, :], idb_h[:, :], "x")
    c_gw = DMA("sy", gwf[:, :].rearrange("p (dc e) -> p dc e", e=8),
               wg_h.rearrange("(dc p) e -> p dc e", p=128), "x")
    c_wrtf = DMA("sy", wrtf[:, :].rearrange("p (dc e) -> p dc e", e=8),
                 wrt_h.rearrange("(dc p) e -> p dc e", p=128), "x")
    DMA("sy", bgbrst[:, 0:8], bg_h.unsqueeze(0), "x")
    c_bgbrst = DMA("sy", bgbrst[:, 8:16], br_h.unsqueeze(0), "x")

    c_b2 = DMA("gp", bias_stack[0:8, :], b2_h[:, :], "w")
    c_be = DMA("gp", bias_stack[8:16, :], be_h[:, :], "w")

    c_b1l = DMA("gp", b1st[:, :].rearrange("hc (e p) -> hc e p", p=128),
                b1_h.rearrange("e (hc p) -> hc e p", p=128), "w")
    W("te", "w", c_b1l)
    W("te", "x", c_ident)
    for e in range(NEXP):
        I("te", lambda E, e=e: E.matmul(psD[:, e * 16 : e * 16 + 16],
                                        b1st[:, e * 128 : (e + 1) * 128],
                                        ident[0:16, 0:16], start=(e == 0), stop=(e == NEXP - 1),
                                        skip_group_check=True))
    c_b1tr = INC("te", "pe")
    W("sc", "pe", c_b1tr)
    I("sc", lambda E: E.copy(b1sb[:, :], psD[:, 0:128]))
    c_b1_done = INC("sc", "act")

    W("ve", "x", max(c_gw, c_bgbrst, c_wrtf))
    gwf_v = gwf[:, :].rearrange("p (dc e) -> p dc e", e=8)
    wrtf_v = wrtf[:, :].rearrange("p (dc e) -> p dc e", e=8)
    gate_v = gate_w[:, :].rearrange("p (dc c) -> p dc c", c=32)
    I("ve", lambda E: E.tensor_copy(gate_v[:, :, 0:8], gwf_v))
    I("ve", lambda E: E.tensor_copy(gate_v[:, :, 8:16], wrtf_v))
    SP()
    I("ve", lambda E: E.tensor_tensor(gate_v[:, :, 16:24], gwf_v, gate_v[:, :, 0:8], op=OP.subtract))
    I("ve", lambda E: E.tensor_tensor(gate_v[:, :, 24:32], wrtf_v, gate_v[:, :, 8:16], op=OP.subtract))
    I("ve", lambda E: E.tensor_copy(bgbr[:, :], bgbrst[:, :]))
    c_init_dve = INC("ve", "dve")

    xst_last = [0, 0]
    st_out_free = [0, 0]
    act_wexp_last = 0
    pe_tr7_done = 0
    pe_ffn7_done = 0
    dve_ffn_last2 = [0, 0]
    c_trlo_prev = 0
    dve_accinit_last = 0

    for s in range(SUPS):
        c_w1h = [[0, 0] for _ in range(NEXP)]
        c_w2 = [0] * NEXP
        if s > 0:
            W("gp", "pe", pe_ffn7_done)
            c_w1h[0][0] = dma_w1(0, 0)
            c_w1h[0][1] = dma_w1(0, 1)
            c_w2[0] = c_w2_next
        else:
            c_w1h[0][0] = dma_w1(0, 0)
            c_w1h[0][1] = dma_w1(0, 1)
            c_w2[0] = dma_w2(0)

        act_xn_evac_last = 0
        c_srT_evac_prev = 0
        dve_gate_evac_prev = 0
        for ci in range(CHUNKS):
            g_ci = s * CHUNKS + ci
            W("sy", "dve", xst_last[ci % 2])
            c_x = DMA("sy", xst[:, (ci % 2) * D : (ci % 2 + 1) * D],
                      x_h[g_ci * 128 : (g_ci + 1) * 128, :], "x")
            xs = xst[:, (ci % 2) * D : (ci % 2 + 1) * D]
            W("ve", "x", c_x)
            if ci == 0:
                W("ve", "st", cnt["st"])
            I("ve", lambda E, xs=xs: E.bn_stats(lnw[:, 0:6], xs[:, 0:512]))
            I("ve", lambda E, xs=xs: E.bn_stats(lnw[:, 6:12], xs[:, 512:1024]))
            SP()
            SP()
            I("ve", lambda E: E.bn_aggr(lnw[:, 2:4], lnw[:, 0:12]))
            c_stats = INC("ve", "dve")
            W("sc", "dve", c_stats)
            I("sc", lambda E: E.activation(lnw[:, 5:6], lnw[:, 3:4], AF.Ln, bias=consts[:, 0:1]))
            I("sc", lambda E: E.activation(lnw[:, 6:7], lnw[:, 5:6], AF.Exp, scale=consts[:, 1:2]))
            c_rstd = INC("sc", "act")
            W("ve", "act", c_rstd)
            W("ve", "pe", c_trlo_prev)
            I("ve", lambda E: E.tensor_scalar(lnw[:, 7:8], lnw[:, 2:3], lnw[:, 6:7], consts[:, 2:3],
                                              op0=OP.mult, op1=OP.mult))
            SP()
            I("ve", lambda E, xs=xs: E.tensor_scalar(xhit[:, :], xs, lnw[:, 6:7], lnw[:, 7:8],
                                                     op0=OP.mult, op1=OP.add))
            I("ve", lambda E, xs=xs: E.tensor_scalar(xnf32[:, :], xs, lnw[:, 6:7], lnw[:, 7:8],
                                                     op0=OP.mult, op1=OP.add))
            SP()
            I("ve", lambda E: E.tensor_tensor(xlot[:, :], xnf32[:, :], xhit[:, :], op=OP.subtract))
            c_xn_tok = INC("ve", "dve")
            xst_last[ci % 2] = c_xn_tok
            W("te", "dve", c_xn_tok)
            if s == 0 and ci == 0:
                W("te", "dve", c_init_dve)
                W("te", "act", c_b1_done)
            if s > 0 and ci == 0:
                W("te", "dve", dve_stt_last2[1])
            if ci >= 1:
                W("te", "act", c_srT_evac_prev)
                W("te", "dve", dve_accinit_last)
                W("te", "dve", dve_gate_evac_prev)
            for dc in range(DC):
                I("te", lambda E, dc=dc: E.matmul(
                    psC[:, dc * 128 : (dc + 1) * 128], xhit[:, dc * 128 : (dc + 1) * 128],
                    ident[:, :], start=(dc % 4 == 0), stop=(dc % 4 == 3), skip_group_check=True))
            c_trhi = INC("te", "pe")
            for dc in range(DC):
                I("te", lambda E, dc=dc: E.matmul(
                    psD[:, dc * 128 : (dc + 1) * 128], xlot[:, dc * 128 : (dc + 1) * 128],
                    ident[:, :], start=(dc % 4 == 0), stop=(dc % 4 == 3), skip_group_check=True))
            c_trlo = INC("te", "pe")
            c_trlo_prev = c_trlo
            W("sc", "pe", c_trhi)
            xn_dst = xn_fm[:, :].rearrange("p (dc n) -> p dc n", n=NSUP)[:, :, ci * 128 : (ci + 1) * 128]
            I("sc", lambda E, xn_dst=xn_dst: E.copy(xn_dst, psC[:, :].rearrange("p (dc n) -> p dc n", n=128)))
            c_xnhi_evac = INC("sc", "act")
            act_xn_evac_last = c_xnhi_evac
            W("sc", "pe", c_trlo)
            if True:
                W("sc", "pe", c_trlo)
            I("sc", lambda E: E.copy(xlofm[:, :], psD[:, :]))
            c_xnlo_evac = INC("sc", "act")
            W("te", "act", c_xnlo_evac)
            for dc in range(DC):
                I("te", lambda E, dc=dc, ci=ci: E.matmul(
                    psC[:, 0:32],
                    xn_fm[:, dc * NSUP + ci * 128 : dc * NSUP + (ci + 1) * 128],
                    gate_w[:, dc * 32 : dc * 32 + 32],
                    start=(dc == 0), stop=False, skip_group_check=True))
            for dc in range(DC):
                I("te", lambda E, dc=dc: E.matmul(
                    psC[:, 32:48], xlofm[:, dc * 128 : (dc + 1) * 128],
                    gate_w[:, dc * 32 : dc * 32 + 16],
                    start=False, stop=False, skip_group_check=True))
            I("te", lambda E: E.matmul(psC[:, 0:16], ones_row[:, :], bgbr[:, :],
                                       start=False, stop=True, skip_group_check=True))
            c_gate = INC("te", "pe")
            W("ve", "pe", c_gate)
            I("ve", lambda E: E.tensor_copy(gw48[:, :], psC[:, 0:48]))
            dve_gate_evac_prev = INC("ve", "dve")
            SP()
            I("ve", lambda E: E.tensor_tensor(lnw[:, 16:24], gw48[:, 16:24], gw48[:, 32:40], op=OP.add))
            I("ve", lambda E: E.tensor_tensor(lnw[:, 24:32], gw48[:, 24:32], gw48[:, 40:48], op=OP.add))
            I("ve", lambda E: E.tensor_tensor(lnw[:, 32:40], lnw[:, 16:24], gw48[:, 0:8], op=OP.add))
            I("ve", lambda E: E.tensor_tensor(lnw[:, 40:48], lnw[:, 24:32], gw48[:, 8:16], op=OP.add))
            I("ve", lambda E: E.reduce_max(lnw[:, 48:49], lnw[:, 32:40], axis=AX.X, negate=True))
            I("ve", lambda E: E.reduce_max(lnw[:, 49:50], lnw[:, 40:48], axis=AX.X, negate=True))
            I("ve", lambda E: E.reduce_max(lnw[:, 50:51], lnw[:, 32:40], axis=AX.X))
            c_glog = INC("ve", "dve")
            W("sc", "dve", c_glog)
            I("sc", lambda E: E.activation(lnw[:, 56:64], lnw[:, 32:40], AF.Exp, bias=lnw[:, 48:49]))
            I("sc", lambda E: E.activation(lnw[:, 64:72], lnw[:, 40:48], AF.Exp, bias=lnw[:, 49:50]))
            c_exps = INC("sc", "act")
            W("ve", "act", c_exps)
            I("ve", lambda E: E.reduce_sum(lnw[:, 51:52], lnw[:, 56:64], axis=AX.X))
            I("ve", lambda E: E.reduce_sum(lnw[:, 52:53], lnw[:, 64:72], axis=AX.X))
            I("ve", lambda E: E.reciprocal(lnw[:, 53:54], lnw[:, 51:52]))
            I("ve", lambda E: E.reciprocal(lnw[:, 54:55], lnw[:, 52:53]))
            I("ve", lambda E: E.tensor_scalar(lnw[:, 72:80], lnw[:, 32:40], lnw[:, 50:51], None,
                                              op0=OP.is_equal))
            I("ve", lambda E: E.tensor_scalar(rwbuf[:, 0:8], lnw[:, 64:72], lnw[:, 54:55], None,
                                              op0=OP.mult))
            I("ve", lambda E, ci=ci: E.tensor_scalar(s_sb[:, ci * 16 : ci * 16 + 8], lnw[:, 72:80],
                                                     lnw[:, 53:54], None, op0=OP.mult))
            I("ve", lambda E: E.reduce_max(rwbuf[:, 8:9], rwbuf[:, 0:8], axis=AX.X))
            SP()
            I("ve", lambda E: E.tensor_scalar(rwbuf[:, 16:24], rwbuf[:, 0:8], rwbuf[:, 8:9], None,
                                              op0=OP.is_lt))
            SP()
            I("ve", lambda E: E.tensor_tensor(rwbuf[:, 24:32], rwbuf[:, 0:8], rwbuf[:, 16:24], op=OP.mult))
            SP()
            I("ve", lambda E: E.reduce_max(rwbuf[:, 9:10], rwbuf[:, 24:32], axis=AX.X))
            SP()
            I("ve", lambda E: E.tensor_scalar(rwbuf[:, 32:40], rwbuf[:, 0:8], rwbuf[:, 9:10], None,
                                              op0=OP.is_ge))
            SP()
            I("ve", lambda E, ci=ci: E.tensor_tensor(s_sb[:, ci * 16 + 8 : ci * 16 + 16], rwbuf[:, 0:8],
                                                     rwbuf[:, 32:40], op=OP.mult))
            c_sr = INC("ve", "dve")
            W("te", "dve", c_sr)
            I("te", lambda E, ci=ci: E.matmul(psC[0:16, 512:640], s_sb[:, ci * 16 : ci * 16 + 16],
                                              ident[:, :], start=True, stop=True))
            c_srT = INC("te", "pe")
            W("sc", "pe", c_srT)
            I("sc", lambda E, ci=ci: E.copy(srT[:, ci * 128 : (ci + 1) * 128], psC[0:16, 512:640]))
            c_srT_evac = INC("sc", "act")
            c_srT_evac_prev = c_srT_evac
            W("te", "act", c_srT_evac)
            if s == 0 and ci == 0:
                W("te", "w", max(c_b2, c_be))
            I("te", lambda E, ci=ci: E.matmul(psD[:, 0:512], srT[:, ci * 128 : (ci + 1) * 128],
                                              bias_stack[:, 0:512], start=True, stop=True))
            I("te", lambda E, ci=ci: E.matmul(psD[:, 512:1024], srT[:, ci * 128 : (ci + 1) * 128],
                                              bias_stack[:, 512:1024], start=True, stop=True))
            c_bias = INC("te", "pe")
            W("ve", "pe", c_bias)
            I("ve", lambda E, ci=ci: E.tensor_copy(acc[:, ci * D : (ci + 1) * D], psD[:, :]))
            dve_accinit_last = INC("ve", "dve")

        if debug and s == 0:
            W("sy", "act", act_xn_evac_last)
            W("sy", "dve", dve_accinit_last)
            DMA("sy", dbg_xn_h, xn_fm[:, :], "st")
            DMA("sy", dbg_s_h, s_sb[:, :], "st")
            c_dump1 = DMA("sy", dbg_acc0_h, acc[:, :], "st")
            W("ve", "st", c_dump1)
            W("te", "st", c_dump1)
        pair_hist = []
        act_pair_evac = {}
        mm2_hist = []
        dve_stt = {}
        act_relu_nr = 0
        for e in range(NEXP):
            for nr in range(NR):
                for hcp in range(8):
                    ps1 = psA if hcp % 2 == 0 else psB
                    if nr == 0 and hcp == 0:
                        W("te", "w", c_w1h[e][0])
                    if nr == 0 and hcp == 4:
                        W("te", "w", c_w1h[e][1])
                    if e == 0 and nr == 0 and hcp == 0:
                        W("te", "act", act_xn_evac_last)
                    if len(pair_hist) >= 2:
                        W("te", "act", act_pair_evac[pair_hist[-2]])
                    elif s > 0:
                        W("te", "dve", dve_ffn_last2[len(pair_hist)])
                    for hh in range(2):
                        hc = hcp * 2 + hh
                        for dc in range(DC):
                            I("te", lambda E, hc=hc, dc=dc, hh=hh, nr=nr, ps1=ps1: E.matmul(
                                ps1[:, hh * 512 : (hh + 1) * 512],
                                w1sb[:, dc * H + hc * 128 : dc * H + (hc + 1) * 128],
                                xn_fm[:, dc * NSUP + nr * 512 : dc * NSUP + (nr + 1) * 512],
                                start=(dc == 0), stop=(dc == DC - 1)))
                    c_pair = INC("te", "pe")
                    if nr == NR - 1 and hcp == 3:
                        pe_w1h0_free = c_pair
                    if nr == NR - 1 and hcp == 7:
                        pe_w1h1_free = c_pair
                    W("sc", "pe", c_pair)
                    for hh in range(2):
                        hc = hcp * 2 + hh
                        I("sc", lambda E, e=e, hc=hc, hh=hh, ps1=ps1: E.activation(
                            hid_sb[:, hc * 512 : (hc + 1) * 512],
                            ps1[:, hh * 512 : (hh + 1) * 512], AF.Relu,
                            bias=b1sb[:, e * HC + hc : e * HC + hc + 1]))
                    key = (e, nr, hcp)
                    act_pair_evac[key] = INC("sc", "act")
                    pair_hist.append(key)
                act_relu_nr = act_pair_evac[(e, nr, 7)]
                if debug and s == 0 and e == 0 and nr == 0:
                    W("sy", "act", act_relu_nr)
                    c_dump_h = DMA("sy", dbg_hid_h, hid_sb[:, :], "st")
                    W("sc", "st", c_dump_h)
                for ns in range(4):
                    ci = nr * 4 + ns
                    ps2 = psC if ns % 2 == 0 else psD
                    if nr == 0 and ns == 0:
                        W("te", "w", c_w2[e])
                    if len(mm2_hist) >= 2:
                        W("te", "dve", dve_stt[mm2_hist[-2]])
                    else:
                        W("te", "dve", dve_accinit_last)
                    for hc in range(HC):
                        if ns == 0:
                            W("te", "act", act_pair_evac[(e, nr, hc // 2)])
                        I("te", lambda E, hc=hc, ns=ns, ps2=ps2: E.matmul(
                            ps2[:, 0:512],
                            hid_sb[:, hc * 512 + ns * 128 : hc * 512 + ns * 128 + 128],
                            w2sb[:, hc * D : hc * D + 512],
                            start=(hc == 0), stop=(hc == HC - 1)))
                        I("te", lambda E, hc=hc, ns=ns, ps2=ps2: E.matmul(
                            ps2[:, 512:1024],
                            hid_sb[:, hc * 512 + ns * 128 : hc * 512 + ns * 128 + 128],
                            w2sb[:, hc * D + 512 : hc * D + 1024],
                            start=(hc == 0), stop=(hc == HC - 1)))
                    key2 = (e, nr, ns)
                    c_mm2 = INC("te", "pe")
                    mm2_hist.append(key2)
                    W("ve", "pe", c_mm2)
                    I("ve", lambda E, ci=ci, e=e, ps2=ps2: E.scalar_tensor_tensor(
                        acc[:, ci * D : (ci + 1) * D], ps2[:, :],
                        s_sb[:, ci * 16 + e : ci * 16 + e + 1],
                        acc[:, ci * D : (ci + 1) * D], op0=OP.mult, op1=OP.add))
                    dve_stt[key2] = INC("ve", "dve")
            pe_mm2_done_e = cnt["pe"]
            if e < NEXP - 1:
                W("gp", "pe", pe_w1h0_free)
                c_w1h[e + 1][0] = dma_w1(e + 1, 0)
                W("gp", "pe", pe_w1h1_free)
                c_w1h[e + 1][1] = dma_w1(e + 1, 1)
                W("gp", "pe", pe_mm2_done_e)
                c_w2[e + 1] = dma_w2(e + 1)
            else:
                pe_w1_all_free = pe_w1h1_free

        if debug and s == 0:
            W("sy", "dve", cnt["dve"])
            c_dump2 = DMA("sy", dbg_accA_h, acc[:, :], "st")
            W("ve", "st", c_dump2)
        dve_stt_last2 = [dve_stt[(NEXP - 1, NR - 1, 2)], dve_stt[(NEXP - 1, NR - 1, 3)]]
        W("gp", "pe", pe_w1_all_free)
        c_wet = [0] * NEXP
        c_wet[0] = dma_wet(0)
        c_wet[1] = dma_wet(1)
        if s + 1 < SUPS:
            W("gp", "pe", cnt["pe"])
            c_w2_next = dma_w2(0)
        ffn_hist = []
        dve_ffn = {}
        pe_ffn_done_e = [0] * NEXP
        for e in range(NEXP):
            W("te", "w", c_wet[e])
            for ci in range(CHUNKS):
                ps3 = psA if ci % 2 == 0 else psB
                if len(ffn_hist) >= 2:
                    W("te", "dve", dve_ffn[ffn_hist[-2]])
                elif e == 0:
                    W("te", "act", act_pair_evac[(NEXP - 1, NR - 1, 6 + ci)])
                for dc in range(DC):
                    I("te", lambda E, dc=dc, ci=ci, e=e, ps3=ps3: E.matmul(
                        ps3[:, 0:512],
                        xn_fm[:, dc * NSUP + ci * 128 : dc * NSUP + (ci + 1) * 128],
                        weT[e % 2][:, dc * D : dc * D + 512],
                        start=(dc == 0), stop=(dc == DC - 1)))
                    I("te", lambda E, dc=dc, ci=ci, e=e, ps3=ps3: E.matmul(
                        ps3[:, 512:1024],
                        xn_fm[:, dc * NSUP + ci * 128 : dc * NSUP + (ci + 1) * 128],
                        weT[e % 2][:, dc * D + 512 : dc * D + 1024],
                        start=(dc == 0), stop=(dc == DC - 1)))
                key3 = (e, ci)
                c_ffn = INC("te", "pe")
                ffn_hist.append(key3)
                W("ve", "pe", c_ffn)
                I("ve", lambda E, ci=ci, e=e, ps3=ps3: E.scalar_tensor_tensor(
                    acc[:, ci * D : (ci + 1) * D], ps3[:, :],
                    s_sb[:, ci * 16 + 8 + e : ci * 16 + 8 + e + 1],
                    acc[:, ci * D : (ci + 1) * D], op0=OP.mult, op1=OP.add))
                dve_ffn[key3] = INC("ve", "dve")
                if e == NEXP - 1:
                    g_ci = s * CHUNKS + ci
                    W("sy", "dve", xst_last[ci % 2])
                    c_x = DMA("sy", xst[:, (ci % 2) * D : (ci % 2 + 1) * D],
                              x_h[g_ci * 128 : (g_ci + 1) * 128, :], "x")
                    xs = xst[:, (ci % 2) * D : (ci % 2 + 1) * D]
                    od = outst[:, (ci % 2) * D : (ci % 2 + 1) * D]
                    W("ve", "x", c_x)
                    W("ve", "st", st_out_free[ci % 2])
                    I("ve", lambda E, xs=xs, od=od, ci=ci: E.tensor_tensor(
                        od, xs, acc[:, ci * D : (ci + 1) * D], op=OP.add))
                    c_res = INC("ve", "dve")
                    xst_last[ci % 2] = c_res
                    W("sy", "dve", c_res)
                    st_out_free[ci % 2] = DMA("sy", out_h[g_ci * 128 : (g_ci + 1) * 128, :], od, "st")
            pe_ffn_done_e[e] = cnt["pe"]
            if e + 2 < NEXP:
                W("gp", "pe", pe_ffn_done_e[e])
                c_wet[e + 2] = dma_wet(e + 2)
        pe_ffn7_done = cnt["pe"]
        act_wexp_last = 0
        dve_ffn_last2 = [dve_ffn[(NEXP - 1, CHUNKS - 2)], dve_ffn[(NEXP - 1, CHUNKS - 1)]]

    W("sy", "st", cnt["st"])

    def replay(engobj, key):
        for item in ops[key]:
            kind = item[0]
            if kind == "call":
                item[1](engobj)
            elif kind == "inccall":
                item[1](engobj).then_inc(item[2], item[3])
            elif kind == "wait":
                engobj.wait_ge(item[1], item[2])
            elif kind == "dma":
                engobj.dma_start(item[1], item[2], **item[4]).then_inc(item[3], 16)

    block = EN(nc.Block())

    @block.tensor
    def _(te):
        replay(te, "te")

    @block.vector
    def _(ve):
        replay(ve, "ve")

    @block.scalar
    def _(sc):
        replay(sc, "sc")

    @block.gpsimd
    def _(gp):
        replay(gp, "gp")

    @block.sync
    def _(sy):
        replay(sy, "sy")

    ctx.close()
    return nc


_prebuilt = None


def kernel(L0_attn_x, L0_attn_Wg, L0_attn_bg, L0_attn_W1, L0_attn_b1, L0_attn_W2,
           L0_attn_b2, L0_ffn_W_experts, L0_ffn_b_experts, L0_ffn_W_router, L0_ffn_b_router):
    global _prebuilt
    x = np.ascontiguousarray(np.asarray(L0_attn_x, dtype=np.float32).reshape(NTOK, D))
    shards = np.split(x, NCORE, axis=0)
    common = dict(
        wg=np.ascontiguousarray(L0_attn_Wg, dtype=np.float32),
        bg=np.ascontiguousarray(L0_attn_bg, dtype=np.float32),
        w1=np.ascontiguousarray(L0_attn_W1, dtype=np.float32),
        b1=np.ascontiguousarray(L0_attn_b1, dtype=np.float32),
        w2=np.ascontiguousarray(L0_attn_W2, dtype=np.float32),
        b2=np.ascontiguousarray(L0_attn_b2, dtype=np.float32),
        wet=np.ascontiguousarray(np.asarray(L0_ffn_W_experts, dtype=np.float32).transpose(0, 2, 1)),
        be=np.ascontiguousarray(L0_ffn_b_experts, dtype=np.float32),
        wr=np.ascontiguousarray(L0_ffn_W_router, dtype=np.float32),
        br=np.ascontiguousarray(L0_ffn_b_router, dtype=np.float32),
        idb=np.eye(128, dtype=ml_dtypes.bfloat16),
        wrt=np.ascontiguousarray(np.asarray(L0_ffn_W_router, dtype=np.float32).T),
    )
    if _prebuilt is None:
        _prebuilt = build_program()
    in_maps = [dict(common, x=sh) for sh in shards]
    res = run_bass_kernel_spmd(_prebuilt, in_maps, core_ids=list(range(NCORE)))
    out = np.concatenate([r["out"] for r in res.results], axis=0)
    return out.reshape(2, 16384, D)


if __name__ == "__main__":
    build_program()
    print("built ok")
